# version 04 (final)
# baseline (speedup 1.0000x reference)
import contextlib
import numpy as np

import concourse.bass as bass
import concourse.mybir as mybir
from concourse.bass_utils import run_bass_kernel_spmd

F32 = mybir.dt.float32
F32R = mybir.dt.float32r
BF16 = mybir.dt.bfloat16
I32 = mybir.dt.int32
I16 = mybir.dt.int16
U16 = mybir.dt.uint16
AF = mybir.ActivationFunctionType
ALU = mybir.AluOpType
AX = mybir.AxisListType.X

B, S, D = 4, 4096, 1024
T = 2048
DT = D // 128
CW = 512
CH = T // CW
I3 = 3072
IT = I3 // 128
H4 = 4096
HT = H4 // 128
QMAX = 32767.0
INV_D = 1.0 / D

TAPS = []
L0_ONLY = False
SPLIT1 = True


class Sem:
    def __init__(self, es, nc, name):
        self.h = es.enter_context(nc.semaphore(name))
        self.n = 0

    def bump(self, instr, k=1):
        instr.then_inc(self.h, k)
        self.n += k
        return self.n


class P:
    pass


def bf(p, off_kb, n_f32):
    return p.Bx[:, off_kb * 256: off_kb * 256 + n_f32]


def build_program(taps=()):
    nc = bass.Bass("TRN2", target_bir_lowering=False, debug=False)
    p = P()
    p.nc = nc
    p.tap_names = list(taps)
    p.tap_out = {}
    dram = {}

    def din(name, shape, dt=F32):
        dram[name] = nc.dram_tensor(name, shape, dt, kind="ExternalInput").ap()

    din("xT", [D, T])
    din("l0_wk", [D, D], F32R); din("l0_wv", [D, D], F32R); din("l0_wr", [D, D], F32R)
    din("l0_wo", [D, D]); din("l0_wfc", [D, H4], F32R); din("l0_wmlp", [H4, D])
    din("l0_vec", [1, 4 * D])
    din("l1_wq", [D, D], F32R); din("l1_wk", [D, D], F32R); din("l1_wv", [D, D], F32R)
    din("l1_wo", [D, D])
    for L in (0, 1):
        din(f"l{L}_fu_my", [I3 // 2, D])
        din(f"l{L}_fg_my", [I3 // 2, D])
        din(f"l{L}_fd_my", [D // 2, I3])
    din("c_ident", [128, 128], BF16)
    din("c_ident32", [128, 128])
    din("c_masks", [128, 2048], BF16)

    p.out = nc.dram_tensor("yT", [D, T], F32, kind="ExternalOutput").ap()
    p.dbg = nc.dram_tensor("dbg_row", [1, 256], F32, kind="ExternalOutput").ap()
    for nm in p.tap_names:
        if nm.startswith("D_"):
            continue
        if nm.startswith("b_"):
            p.tap_out[nm] = nc.dram_tensor("tap_" + nm, [D, T // 2], F32, kind="ExternalOutput").ap()
        else:
            p.tap_out[nm] = nc.dram_tensor("tap_" + nm, [D, T], F32, kind="ExternalOutput").ap()

    if SPLIT1:
        p.x1T = nc.dram_tensor("x1T", [D, T], F32, kind="ExternalOutput").ap()
        for onm in ("q_out", "k_out", "v_out"):
            dram[onm] = nc.dram_tensor(onm, [D, T // 2], F32, kind="ExternalOutput").ap()
        dram["g_out"] = nc.dram_tensor("g_out", [D, T], F32, kind="ExternalOutput").ap()
    else:
        p.x1T = nc.dram_tensor("x1T", [D, T], F32).ap()
    p.sp_up = nc.dram_tensor("sp_up", [I3, T], F32).ap()
    p.sp_gi = nc.dram_tensor("sp_gi", [I3, T], I16).ap()
    p.sp_su = nc.dram_tensor("sp_su", [I3, T], U16).ap()
    p.sp_ops = nc.dram_tensor("sp_ops", [I3, T], F32).ap()
    p.sp_obf = nc.dram_tensor("sp_obf", [I3, T], BF16).ap()
    p.sh_state = nc.dram_tensor("sh_state", [4, D], F32, addr_space="Shared").ap()
    p.sh_wq = {}
    p.sh_wqT = {}
    p.lwT = {}
    for L in (0, 1):
        p.sh_wqT[(L, "fu")] = nc.dram_tensor(f"shwuT{L}", [D, I3], BF16, addr_space="Shared").ap()
        p.sh_wqT[(L, "fg")] = nc.dram_tensor(f"shwgT{L}", [D, I3], BF16, addr_space="Shared").ap()
        p.sh_wqT[(L, "fd")] = nc.dram_tensor(f"shwdT{L}", [I3, D], BF16, addr_space="Shared").ap()
        p.lwT[(L, "fu")] = nc.dram_tensor(f"lwuT{L}", [D, I3], BF16).ap()
        p.lwT[(L, "fg")] = nc.dram_tensor(f"lwgT{L}", [D, I3], BF16).ap()
        p.lwT[(L, "fd")] = nc.dram_tensor(f"lwdT{L}", [I3, D], BF16).ap()
    p.sh_qk = nc.dram_tensor("sh_qk", [2, 1024, T], BF16, addr_space="Shared").ap()
    p.sh_v = nc.dram_tensor("sh_v", [2, T, 512], BF16, addr_space="Shared").ap()
    p.sh_vs = nc.dram_tensor("sh_vs", [2, 32, 256], BF16, addr_space="Shared").ap()
    p.sh_o = nc.dram_tensor("sh_o", [2, 512, T], BF16, addr_space="Shared").ap()
    NAR = 14
    p.ar_in = [nc.dram_tensor(f"ari{i}", [1, 2048], F32).ap() for i in range(NAR)]
    p.ar_out = [nc.dram_tensor(f"aro{i}", [1, 2048], F32).ap() for i in range(NAR)]
    p.ar_k = 0
    p.dram = dram

    es = contextlib.ExitStack()

    def sb(name, shape, dt):
        return es.enter_context(nc.sbuf_tensor(name, shape, dt))

    with es:
        p.ident = sb("ident", [128, 128], BF16)
        p.ones = sb("ones", [128, 128], F32)
        p.vecs = sb("vecs", [128, 64], F32)
        p.qsc = sb("qsc", [128, 16], F32)
        p.eps = sb("eps", [128, 8], F32)
        p.lmaxc = sb("lmaxc", [128, 40], F32)
        p.state = sb("state", [128, 16], F32)
        p.srow = sb("srow", [1, 32], F32)
        p.gath = sb("gath", [128, 8], F32)
        p.ident32 = sb("ident32", [128, 128], F32)
        p.mu = sb("mu", [128, T], F32)
        p.sp = sb("sp", [128, T], F32)
        p.c2 = sb("c2", [128, T], F32)
        p.A = [sb(f"A{j}", [128, T], F32) for j in range(DT)]
        p.Bx = sb("Bx", [128, 10240], F32)
        p.wf = [sb("wfA", [128, 1024], F32R), sb("wfB", [128, 1024], F32R)]
        p.rw = [sb(f"rw{j}", [128, T], BF16) for j in range(DT)]
        p.xq = [sb(f"xq{j}", [128, T], BF16) for j in range(DT)]
        p.pb = [es.enter_context(nc.psum_tensor(f"pb{i}", [128, CW], F32)) for i in range(8)]

        for nm in ("s_ld", "s_st", "s_gd", "s_cc", "s_mm", "s_ve", "s_ac", "s_gp",
                   "s_boot", "s_x", "s_wA", "s_wB", "s_qA", "s_qB", "s_t",
                   "s_ob", "s_wd0", "s_wd1", "s_xc0", "s_xc1", "s_y0", "s_y1",
                   "s_sg0", "s_sg1", "s_ts", "s_ar", "s_wo0", "s_wo1",
                   "s_wm0", "s_wm1", "s_dqg", "s_z", "s_tap", "s_dq", "s_bn"):
            setattr(p, nm, Sem(es, nc, nm))

        p.block = es.enter_context(nc.Block())
        build_all(p)

    return nc, p


def tap(p, name, tiles):
    if name not in p.tap_names:
        return

    @p.block.sync
    def _(sync):
        sync.wait_ge(p.s_ve.h, p.s_ve.n)
        for j, t in enumerate(tiles):
            src = t.bitcast(F32) if name.startswith("b_") else t
            sync.dma_start(out=p.tap_out[name][128 * j:128 * (j + 1), :], in_=src).then_inc(p.s_tap.h, 16)
            p.s_tap.n += 16


def tap_dram(p, name, src_ap, shape, dt):
    if name not in p.tap_names:
        return
    if dt == BF16:
        shape = [shape[0], shape[1] // 2]
        src_ap = src_ap.bitcast(F32)
        dt = F32
    p.tap_out[name] = p.nc.dram_tensor("tap_" + name, shape, dt, kind="ExternalOutput").ap()

    @p.block.sync
    def _(sync):
        sync.wait_ge(p.s_ts.h, p.s_ts.n)
        sync.wait_ge(p.s_dq.h, p.s_dq.n)
        sync.dma_start(out=p.tap_out[name][:, :], in_=src_ap).then_inc(p.s_tap.h, 16)
        p.s_tap.n += 16


def ar_gather(p, payload_cols, dst_cols, wait_before=()):
    blk = p.block
    idx = p.ar_k
    p.ar_k += 1

    @blk.gpsimd
    def _(g):
        for sh, val in wait_before:
            g.wait_ge(sh, val)
        for q, col in enumerate(payload_cols):
            for arm in g.switch_core_id(n=128):
                off = (arm.logical % 8) * 128
                g.dma_start(out=p.ar_in[idx][0:1, 1024 * q + off:1024 * q + off + 128]
                            .rearrange("a (q2 o) -> (a q2) o", q2=128),
                            in_=col).then_inc(p.s_ar.h, 16)
            p.s_ar.n += 16
        g.wait_ge(p.s_ar.h, p.s_ar.n)
        inst = g.collective_compute(
            "AllReduce", ALU.add, replica_groups=[list(range(8))],
            ins=[p.ar_in[idx].opt()], outs=[p.ar_out[idx].opt()])
        p.s_cc.bump(inst, 1)
        g.wait_ge(p.s_cc.h, p.s_cc.n)

    for q in range(len(payload_cols)):
        @blk.gpsimd
        def _(g, q=q):
            if q > 0:
                g.wait_ge(p.s_ve.h, p.ar_ve)
            for sl in range(8):
                g.dma_start(out=p.gath[:, sl:sl + 1],
                            in_=p.ar_out[idx][0:1, 1024 * q + 128 * sl:1024 * q + 128 * (sl + 1)]
                            .rearrange("a (q2 o) -> (a q2) o", q2=128)).then_inc(p.s_ar.h, 16)
                p.s_ar.n += 16
        m_g = p.s_ar.n

        @blk.vector
        def _(v, q=q, m_g=m_g):
            v.wait_ge(p.s_ar.h, m_g)
            inst = v.tensor_reduce(out=p.lmaxc[:, 33:34], in_=p.gath[:, :],
                                   axis=AX, op=ALU.max)
            p.s_ve.bump(inst)
        m_col = p.s_ve.n

        @blk.tensor
        def _(t, m_col=m_col):
            t.wait_ge(p.s_ve.h, m_col)
            if p.bc_guard:
                t.wait_ge(p.s_ve.h, p.bc_guard)
            inst = t.matmul(p.pb[5][0:1, 0:128], p.lmaxc[:, 33:34], p.ident32[:, :],
                            start=True, stop=True)
            p.s_mm.bump(inst)
        m_tr = p.s_mm.n

        @blk.vector
        def _(v, q=q, m_tr=m_tr):
            v.wait_ge(p.s_mm.h, m_tr)
            d0 = dst_cols[q]
            inst = v.tensor_reduce(out=p.srow[0:1, d0:d0 + 1], in_=p.pb[5][0:1, 0:128],
                                   axis=AX, op=ALU.max)
            p.s_ve.bump(inst)
        p.ar_ve = p.s_ve.n
        p.bc_guard = p.s_ve.n
    return p.s_ve.n


def bcast_cols(p, src_cols, dst_cols, m_ve=None):
    blk = p.block
    s0, s1 = src_cols
    d0, d1 = dst_cols
    w = s1 - s0
    m_in = p.s_ve.n if m_ve is None else m_ve

    @blk.tensor
    def _(t):
        t.wait_ge(p.s_ve.h, m_in)
        if p.bc_guard:
            t.wait_ge(p.s_ve.h, p.bc_guard)
        inst = t.matmul(p.pb[5][:, 0:w], p.ones[0:1, :],
                        p.srow[0:1, s0:s1], start=True, stop=True)
        p.s_mm.bump(inst)
    m_mm = p.s_mm.n

    @blk.vector
    def _(v):
        v.wait_ge(p.s_mm.h, m_mm)
        inst = v.tensor_copy(p.qsc[:, d0:d1], p.pb[5][:, 0:w])
        p.s_ve.bump(inst)
    p.bc_guard = p.s_ve.n
    return p.s_ve.n


def colblock_load(p, eng, name, o, wbuf, sem, width=128, n_in=None):
    src = p.dram[name]
    n_in = n_in if n_in is not None else src.shape[0] // 128
    for j in range(n_in):
        eng.dma_start(out=wbuf[:, width * j:width * (j + 1)],
                      in_=src[128 * j:128 * (j + 1), width * o:width * (o + 1)]).then_inc(sem.h, 16)
        sem.n += 16
    return sem.n


def build_all(p):
    p.bc_guard = 0
    emit_prologue(p)
    emit_dequant(p, 0)
    emit_stats_xn(p, 0)
    emit_mixer_kv(p)
    emit_ar1_xq(p, 0)
    emit_mixer_tail(p)
    emit_ffn_upgate(p, 0)
    emit_mlp_internal(p)
    emit_ffn_tail(p, 0)
    emit_down_combine(p, 0)
    if L0_ONLY:
        emit_final_wait(p)
        return
    if SPLIT1:
        emit_dequant(p, 1)
        emit_stats_xn(p, 1)
        emit_ar1_xq(p, 1)
        emit_l1_qkv(p)
        emit_ffn_upgate(p, 1)
        emit_ffn_tail(p, 1)
        emit_down_combine(p, 1)
        emit_final_wait(p)
        return
    emit_dequant(p, 1)
    emit_stats_xn(p, 1)
    emit_l1_mixer(p)
    emit_ar1_xq(p, 1)
    emit_l1_attn_all(p)
    emit_ffn_upgate(p, 1)
    emit_l1_wo(p)
    emit_ffn_tail(p, 1)
    emit_down_combine(p, 1)
    emit_final_wait(p)


def emit_prologue(p):
    blk, d = p.block, p.dram

    @blk.sync
    def _(sync):
        sync.dma_start(out=p.ident[:, :], in_=d["c_ident"][:, :]).then_inc(p.s_boot.h, 16)
        p.s_boot.n += 16
        sync.dma_start(out=p.ident32[:, :], in_=d["c_ident32"][:, :]).then_inc(p.s_boot.h, 16)
        p.s_boot.n += 16
        v = d["l0_vec"].rearrange("a (k j q) -> a k j q", k=4, j=DT)
        for k in range(4):
            for j in range(DT):
                sync.dma_start(out=p.vecs[:, 8 * k + j:8 * k + j + 1],
                               in_=v[0:1, k, j, :].rearrange("a q -> q a")).then_inc(p.s_boot.h, 16)
                p.s_boot.n += 16
    p.m_lvec = p.s_boot.n

    @blk.gpsimd
    def _(g):
        g.memset(p.ones[:, :], 1.0)
        g.memset(p.srow[:, :], 0.0)
        g.memset(p.lmaxc[:, :], 0.0)
        g.memset(p.gath[:, :], 0.0)
        g.memset(p.state[:, :], 0.0)
        g.memset(p.qsc[:, :], 0.0)
        g.memset(p.eps[:, 0:1], 1e-5)
        g.memset(p.eps[:, 1:2], 1e-6)
        g.memset(p.eps[:, 2:3], 0.0)
        g.memset(p.eps[:, 3:4], 1.0)
        g.drain()
        for arm in g.switch_core_id(n=128):
            par = arm.logical % 2
            for sl in range(8):
                g.dma_start(out=p.sh_state[2 + par:3 + par, 128 * sl:128 * (sl + 1)]
                            .rearrange("a (q2 o) -> (a q2) o", q2=128),
                            in_=p.gath[:, sl:sl + 1]).then_inc(p.s_z.h, 16)
        p.s_z.n += 16 * 8
        for a in p.ar_in:
            for hh in range(16):
                g.dma_start(out=a[0:1, 128 * hh:128 * (hh + 1)]
                            .rearrange("a (q2 o) -> (a q2) o", q2=128),
                            in_=p.gath[:, 0:1]).then_inc(p.s_z.h, 16)
                p.s_z.n += 16
        g.wait_ge(p.s_z.h, p.s_z.n)
        inst = g.engine_nop()
        p.s_gp.bump(inst)
    p.m_zeroed = ("gp", p.s_gp.n)

    @blk.scalar
    def _(s):
        s.wait_ge(p.s_boot.h, p.m_lvec)
        s.activation(out=p.vecs[:, 32:40], in_=p.vecs[:, 0:8], func=AF.Exp)
        s.drain()
        s.activation(out=p.vecs[:, 40:48], in_=p.vecs[:, 32:40], func=AF.Exp, scale=-1.0)
        s.drain()
        inst = s.activation(out=p.vecs[:, 48:56], in_=p.vecs[:, 8:16], func=AF.Exp)
        p.s_ac.bump(inst)
    p.m_vecs = p.s_ac.n


def emit_dequant(p, L):
    blk, d = p.block, p.dram
    DQW = 256
    base = 9216
    wv_ = p.Bx[:, base:base + DQW]
    iv_ = p.Bx[:, base + 256:base + 256 + DQW].bitcast(I32)
    ev_ = p.Bx[:, base + 512:base + 512 + DQW // 2].bitcast(BF16)
    sc_ = p.Bx[:, base + 640:base + 648]
    rc_ = p.Bx[:, base + 648:base + 656]
    evT = p.Bx[:, base + 656:base + 656 + DQW // 2].bitcast(BF16)
    gate_ve = 0 if L == 0 else p.s_ve.n
    jobs = [("fu", I3 // 2, D), ("fg", I3 // 2, D), ("fd", D // 2, I3)]
    m_prev_store = None
    for wname, MYR, I in jobs:
        for it in range(MYR // 128):
            for cb in range(I // DQW):
                m_prev_ve = p.s_ve.n

                @blk.gpsimd
                def _(g, wname=wname, it=it, cb=cb, mps=m_prev_store):
                    if mps is not None:
                        g.wait_ge(p.s_dq.h, mps)
                    g.wait_ge(p.s_gp.h, p.m_zeroed[1])
                    if gate_ve:
                        g.wait_ge(p.s_ve.h, gate_ve)
                    g.dma_start(out=wv_, in_=d[f"l{L}_{wname}_my"][128 * it:128 * (it + 1),
                                                                   DQW * cb:DQW * (cb + 1)]).then_inc(p.s_dqg.h, 16)
                    p.s_dqg.n += 16
                m_ld = p.s_dqg.n

                @blk.vector
                def _(v, m_ld=m_ld):
                    v.wait_ge(p.s_dqg.h, m_ld)
                    wvv = wv_.rearrange("q (a b) -> q a b", b=32)
                    v.tensor_reduce(out=sc_, in_=wvv, axis=AX, op=ALU.max,
                                    apply_absolute_value=True)
                    v.drain()
                    v.tensor_scalar(out=sc_, in0=sc_, scalar1=1.0 / 7.0, op0=ALU.mult,
                                    scalar2=1e-12, op1=ALU.max)
                    v.drain()
                    v.reciprocal(rc_, sc_)
                    v.drain()
                    r_e = rc_.rearrange("q (a b) -> q a b", b=1).broadcast_to((128, 8, 32))
                    s_e = sc_.rearrange("q (a b) -> q a b", b=1).broadcast_to((128, 8, 32))
                    ivv = iv_.rearrange("q (a b) -> q a b", b=32)
                    v.tensor_tensor(out=ivv, in0=wvv, in1=r_e, op=ALU.mult)
                    v.drain()
                    inst = v.scalar_tensor_tensor(
                        out=ev_.rearrange("q (a b) -> q a b", b=32),
                        in0=ivv, scalar=1.0, op0=ALU.mult, in1=s_e, op1=ALU.mult)
                    p.s_ve.bump(inst)
                m_ve = p.s_ve.n

                @blk.tensor
                def _(t, m_ve=m_ve):
                    t.wait_ge(p.s_ve.h, m_ve)
                    for q in range(2):
                        inst = t.matmul(p.pb[5][:, 128 * q:128 * (q + 1)],
                                        ev_[:, 128 * q:128 * (q + 1)],
                                        p.ident[:, :], start=True, stop=True)
                    p.s_mm.bump(inst)
                m_tp = p.s_mm.n

                @blk.vector
                def _(v, m_tp=m_tp):
                    v.wait_ge(p.s_mm.h, m_tp)
                    inst = v.tensor_copy(evT, p.pb[5][:, 0:256])
                    p.s_ve.bump(inst)
                m_evT = p.s_ve.n

                @blk.scalar
                def _(s, wname=wname, it=it, cb=cb, I=I, m_evT=m_evT):
                    s.wait_ge(p.s_ve.h, m_evT)
                    O_half = (I3 if wname != "fd" else D) // 2
                    for arm in s.switch_core_id(n=128):
                        r0 = (arm.logical % 2) * O_half + 128 * it
                        for q in range(2):
                            s.dma_start(out=p.sh_wqT[(L, wname)][DQW * cb + 128 * q:
                                                                 DQW * cb + 128 * (q + 1),
                                                                 r0:r0 + 128],
                                        in_=evT[:, 128 * q:128 * (q + 1)]).then_inc(p.s_dq.h, 16)
                    p.s_dq.n += 32
                m_prev_store = p.s_dq.n
    if not hasattr(p, "m_dq"):
        p.m_dq = {}
    p.m_dq[L] = p.s_dq.n
    tap_dram(p, f"D_wu{L}", p.sh_wqT[(L, "fu")][:, :], [D, I3], BF16)


def emit_stats_xn(p, L):
    blk, d = p.block, p.dram
    xsrc = d["xT"] if L == 0 else p.x1T
    xt = [bf(p, 2 * j, CW) for j in range(DT)]
    x2a, x2b = bf(p, 16, CW), bf(p, 18, CW)
    var = bf(p, 20, CW)
    m_xn_prev = None
    for c in range(CH):
        cs = slice(c * CW, (c + 1) * CW)

        @blk.sync
        def _(sync, c=c, mxp=m_xn_prev):
            if c == 0:
                sync.wait_ge(p.s_ac.h, p.m_vecs)
                if L == 1:
                    sync.wait_ge(p.s_y0.h, p.m_y_done[0])
                    sync.wait_ge(p.s_y1.h, p.m_y_done[1])
            if mxp is not None:
                sync.wait_ge(p.s_ve.h, mxp)
            for j in range(DT):
                sync.dma_start(out=xt[j], in_=xsrc[128 * j:128 * (j + 1), cs]).then_inc(p.s_x.h, 16)
                p.s_x.n += 16
        m_ld = p.s_x.n

        m_sq = []
        S0 = p.s_mm.n

        @blk.scalar
        def _(s, c=c, m_ld=m_ld, S0=S0):
            s.wait_ge(p.s_x.h, m_ld)
            for j in range(DT):
                if j >= 2:
                    s.wait_ge(p.s_mm.h, S0 + (j - 2) + 1)
                buf = x2a if j % 2 == 0 else x2b
                inst = s.activation(out=buf, in_=xt[j], func=AF.Square)
                p.s_ac.bump(inst)
                m_sq.append(p.s_ac.n)

        @blk.tensor
        def _(t, c=c, m_ld=m_ld, m_sq=m_sq, L=L):
            t.wait_ge(p.s_x.h, m_ld)
            if c == 0 and L == 0:
                t.wait_ge(p.s_gp.h, p.m_zeroed[1])
            if c > 0:
                t.wait_ge(p.s_ve.h, p.m_var)
            for j in range(DT):
                t.matmul(p.pb[4][:, :], p.ones[:, :], xt[j],
                         start=(j == 0), stop=(j == DT - 1))
            for j in range(DT):
                t.wait_ge(p.s_ac.h, m_sq[j])
                buf = x2a if j % 2 == 0 else x2b
                inst = t.matmul(p.pb[5][:, :], p.ones[:, :], buf,
                                start=(j == 0), stop=(j == DT - 1))
                p.s_mm.bump(inst)
        m_mm = p.s_mm.n

        @blk.vector
        def _(v, c=c, cs=cs, m_mm=m_mm):
            v.wait_ge(p.s_mm.h, m_mm)
            v.tensor_scalar(out=p.mu[:, cs], in0=p.pb[4][:, :], scalar1=INV_D,
                            scalar2=None, op0=ALU.mult)
            v.drain()
            v.tensor_scalar(out=var, in0=p.pb[5][:, :], scalar1=INV_D,
                            scalar2=None, op0=ALU.mult)
            v.drain()
            v.tensor_mul(p.sp[:, cs], p.mu[:, cs], p.mu[:, cs])
            v.drain()
            inst = v.tensor_sub(var, var, p.sp[:, cs])
            p.s_ve.bump(inst)
        p.m_var = p.s_ve.n

        @blk.scalar
        def _(s, c=c, cs=cs, L=L):
            if c == 0 and L == 0:
                s.wait_ge(p.s_gp.h, p.m_zeroed[1])
            s.wait_ge(p.s_ve.h, p.m_var)
            inst = s.activation(out=p.sp[:, cs], in_=var, func=AF.Sqrt, bias=p.eps[:, 0:1])
            p.s_ac.bump(inst)
        m_sd = p.s_ac.n

        @blk.vector
        def _(v, c=c, cs=cs, m_sd=m_sd, L=L):
            v.wait_ge(p.s_ac.h, m_sd)
            v.reciprocal(p.sp[:, cs], p.sp[:, cs])
            v.drain()
            if L == 0:
                v.tensor_mul(p.c2[:, cs], var, p.sp[:, cs])
                v.drain()
                v.tensor_mul(p.c2[:, cs], p.c2[:, cs], p.sp[:, cs])
                v.drain()
            for j in range(DT):
                v.tensor_sub(p.A[j][:, cs].bitcast(F32R), xt[j], p.mu[:, cs])
                v.drain()
                v.tensor_mul(p.A[j][:, cs].bitcast(F32R), p.A[j][:, cs], p.sp[:, cs])
                v.drain()
                v.tensor_reduce(out=p.lmaxc[:, c * DT + j:c * DT + j + 1],
                                in_=p.A[j][:, cs], axis=AX, op=ALU.max,
                                apply_absolute_value=True)
                v.drain()
            inst = v.engine_nop()
            p.s_ve.bump(inst)
        m_xn_prev = p.s_ve.n

        if L == 0:
            @blk.scalar
            def _(s, c=c, cs=cs, mxp=m_xn_prev):
                s.wait_ge(p.s_ve.h, mxp)
                inst = s.activation(out=p.c2[:, cs], in_=p.c2[:, cs], func=AF.Sqrt,
                                    bias=p.eps[:, 1:2])
                p.s_ac.bump(inst)
            m_c2s = p.s_ac.n

            @blk.vector
            def _(v, c=c, cs=cs, m_c2s=m_c2s):
                v.wait_ge(p.s_ac.h, m_c2s)
                inst = v.reciprocal(p.c2[:, cs], p.c2[:, cs])
                p.s_ve.bump(inst)
            m_xn_prev = p.s_ve.n

    @blk.vector
    def _(v):
        inst = v.tensor_reduce(out=p.lmaxc[:, 32:33], in_=p.lmaxc[:, 0:CH * DT],
                               axis=AX, op=ALU.max)
        p.s_ve.bump(inst)
    p.m_xn = p.s_ve.n
    tap(p, f"xn{L}", [a[:, :] for a in p.A])


def emit_mixer_kv(p):
    blk, d = p.block, p.dram
    wA, wB = p.wf[0], p.wf[1]
    kt, vt, kvt = bf(p, 8, T), bf(p, 16, T), bf(p, 24, T)
    sc0 = p.sp[:, :]
    m_ev_prev = {}

    for o in range(DT):
        for pi, wname in ((0, "l0_wk"), (1, "l0_wv")):
            ki = o * 2 + pi
            wbuf = wA if ki % 2 == 0 else wB

            @blk.sync
            def _(sync, o=o, wname=wname, ki=ki, wbuf=wbuf):
                if ki == 0:
                    sync.wait_ge(p.s_ve.h, p.m_xn)
                if ki >= 2:
                    sync.wait_ge(p.s_mm.h, p.m_kvmm[ki - 2])
                wsem = p.s_wA if ki % 2 == 0 else p.s_wB
                colblock_load(p, sync, wname, o, wbuf, wsem)
            m_w = (p.s_wA if ki % 2 == 0 else p.s_wB).n
            if not hasattr(p, "m_kvmm"):
                p.m_kvmm = {}

            m_out = []

            @blk.tensor
            def _(t, o=o, pi=pi, ki=ki, wbuf=wbuf, m_w=m_w):
                t.wait_ge((p.s_wA if ki % 2 == 0 else p.s_wB).h, m_w)
                for c in range(CH):
                    if (ki - 1, c) in m_ev_prev:
                        t.wait_ge(p.s_ve.h, m_ev_prev[(ki - 1, c)])
                    for j in range(DT):
                        inst = t.matmul(p.pb[c][:, :],
                                        wbuf[:, 128 * j:128 * (j + 1)],
                                        p.A[j][:, c * CW:(c + 1) * CW].bitcast(F32R),
                                        start=(j == 0), stop=(j == DT - 1))
                    p.s_mm.bump(inst)
                    m_out.append(p.s_mm.n)
                p.m_kvmm[ki] = p.s_mm.n

            @blk.vector
            def _(v, o=o, pi=pi, ki=ki, m_out=m_out):
                dst = kt if pi == 0 else vt
                for c in range(CH):
                    cs = slice(c * CW, (c + 1) * CW)
                    v.wait_ge(p.s_mm.h, m_out[c])
                    inst = v.tensor_mul(dst[:, cs], p.pb[c][:, :], p.c2[:, cs])
                    v.drain()
                    p.s_ve.bump(inst)
                    m_ev_prev[(ki, c)] = p.s_ve.n

        @blk.vector
        def _(v, o=o):
            ew = p.vecs[:, 40 + o:41 + o]
            eu = p.vecs[:, 48 + o:49 + o]
            v.tensor_mul(kvt[:, :], kt[:, :], vt[:, :])
            v.drain()
            v.tensor_tensor_scan(out=sc0[:, :], data0=ew.broadcast_to((128, T)),
                                 data1=kvt[:, :], initial=0.0, op0=ALU.mult, op1=ALU.add)
            v.drain()
            v.tensor_copy(p.state[:, o:o + 1], sc0[:, T - 1:T])
            v.drain()
            v.scalar_tensor_tensor(out=p.rw[o][:, 1:T], in0=kvt[:, 1:T], scalar=eu,
                                   op0=ALU.mult, in1=sc0[:, 0:T - 1], op1=ALU.add)
            v.drain()
            inst = v.scalar_tensor_tensor(out=p.rw[o][:, 0:1], in0=kvt[:, 0:1], scalar=eu,
                                          op0=ALU.mult, in1=p.eps[:, 2:3], op1=ALU.add)
            p.s_ve.bump(inst)
    p.m_scan = p.s_ve.n


def emit_ar1_xq(p, L):
    blk = p.block

    if L == 0:
        @blk.gpsimd
        def _(g):
            g.wait_ge(p.s_ve.h, p.m_scan)
            for arm in g.switch_core_id(n=128):
                par = arm.logical % 2
                for j in range(DT):
                    g.dma_start(out=p.sh_state[par:par + 1, 128 * j:128 * (j + 1)]
                                .rearrange("a (q o) -> (a q) o", q=128),
                                in_=p.state[:, j:j + 1]).then_inc(p.s_ar.h, 16)
            p.s_ar.n += 16 * DT
            g.wait_ge(p.s_ar.h, p.s_ar.n)
    ar_gather(p, [p.lmaxc[:, 32:33]], [0],
              wait_before=[(p.s_ve.h, p.m_xn), (p.s_dq.h, p.m_dq[L])])
    m_ar1 = p.ar_ve

    @blk.sync
    def _(sync, L=L, m_ar1=m_ar1):
        sync.wait_ge(p.s_ve.h, m_ar1)
        for wname in ("fu", "fg", "fd"):
            sync.dma_start(out=p.lwT[(L, wname)][:, :],
                           in_=p.sh_wqT[(L, wname)][:, :]).then_inc(p.s_bn.h, 16)
            p.s_bn.n += 16
    if not hasattr(p, "m_bn"):
        p.m_bn = {}
    p.m_bn[L] = p.s_bn.n
    if L == 0:
        @blk.gpsimd
        def _(g):
            g.wait_ge(p.s_ar.h, p.s_ar.n)
            for arm in g.switch_core_id(n=128):
                par = arm.logical % 2
                src = p.sh_state[0:1, :] if par == 1 else p.sh_state[2:3, :]
                for j in range(DT):
                    g.dma_start(out=p.state[:, 8 + j:9 + j],
                                in_=src[0:1, 128 * j:128 * (j + 1)]
                                .rearrange("a (q o) -> (a q) o", q=128)).then_inc(p.s_ar.h, 16)
            p.s_ar.n += 16 * DT
            g.wait_ge(p.s_ar.h, p.s_ar.n)
            inst = g.engine_nop()
            p.s_gp.bump(inst)
        p.m_state_in = p.s_gp.n
    else:
        p.m_state_in = p.s_gp.n

    @blk.vector
    def _(v):
        v.wait_ge(p.s_ve.h, p.ar_ve)
        v.tensor_scalar(out=p.srow[0:1, 1:2], in0=p.srow[0:1, 0:1], scalar1=1.0 / QMAX,
                        op0=ALU.mult, scalar2=1e-12, op1=ALU.max)
        v.drain()
        inst = v.reciprocal(p.srow[0:1, 2:3], p.srow[0:1, 1:2])
        p.s_ve.bump(inst)

    bcast_cols(p, (1, 3), (0, 2))
    m_q = p.bc_guard

    @blk.vector
    def _(v):
        v.wait_ge(p.s_ve.h, m_q)
        ii = p.mu[:, :].bitcast(I32)
        xqf = bf(p, 16, T)
        for j in range(DT):
            v.tensor_scalar(out=ii, in0=p.A[j][:, :], scalar1=p.qsc[:, 1:2],
                            scalar2=None, op0=ALU.mult)
            v.drain()
            v.tensor_scalar(out=xqf, in0=ii, scalar1=p.qsc[:, 0:1],
                            scalar2=None, op0=ALU.mult)
            v.drain()
            v.tensor_copy(p.xq[j][:, :], xqf)
            v.drain()
        inst = v.engine_nop()
        p.s_ve.bump(inst)
    p.m_xq = p.s_ve.n


def emit_mixer_tail(p):
    blk, d = p.block, p.dram
    rt = bf(p, 8, T)
    pw = bf(p, 16, T)
    woA = bf(p, 24, 512).bitcast(BF16)
    woB = bf(p, 26, 512).bitcast(BF16)

    @blk.vector
    def _(v):
        v.wait_ge(p.s_gp.h, p.m_state_in)
        for o in range(DT):
            ew = p.vecs[:, 40 + o:41 + o]
            sin = p.state[:, 8 + o:9 + o]
            v.tensor_tensor_scan(out=pw[:, :], data0=ew.broadcast_to((128, T)),
                                 data1=p.eps[:, 2:3].broadcast_to((128, T)),
                                 initial=1.0, op0=ALU.mult, op1=ALU.add)
            v.drain()
            v.scalar_tensor_tensor(out=p.rw[o][:, 1:T], in0=pw[:, 0:T - 1], scalar=sin,
                                   op0=ALU.mult, in1=p.rw[o][:, 1:T], op1=ALU.add)
            v.drain()
            v.scalar_tensor_tensor(out=p.rw[o][:, 0:1], in0=p.eps[:, 3:4], scalar=sin,
                                   op0=ALU.mult, in1=p.rw[o][:, 0:1], op1=ALU.add)
            inst = v.drain()
            p.s_ve.bump(inst)
    p.m_corr = p.s_ve.n

    wA, wB = p.wf[0], p.wf[1]
    m_rev = {}
    for o in range(DT):
        wbuf = wA if o % 2 == 0 else wB

        @blk.sync
        def _(sync, o=o, wbuf=wbuf):
            if o == 0:
                sync.wait_ge(p.s_ve.h, p.m_corr)
            if o >= 2:
                sync.wait_ge(p.s_mm.h, p.m_rmm[o - 2])
            colblock_load(p, sync, "l0_wr", o, wbuf, p.s_wA if o % 2 == 0 else p.s_wB)
        m_w = (p.s_wA if o % 2 == 0 else p.s_wB).n
        if not hasattr(p, "m_rmm"):
            p.m_rmm = {}
        m_out = []

        @blk.tensor
        def _(t, o=o, wbuf=wbuf, m_w=m_w):
            t.wait_ge((p.s_wA if o % 2 == 0 else p.s_wB).h, m_w)
            if o == 0:
                t.wait_ge(p.s_ve.h, p.m_scan)
            for c in range(CH):
                if (o - 1, c) in m_rev:
                    t.wait_ge(p.s_ve.h, m_rev[(o - 1, c)])
                for j in range(DT):
                    inst = t.matmul(p.pb[c][:, :],
                                    wbuf[:, 128 * j:128 * (j + 1)],
                                    p.A[j][:, c * CW:(c + 1) * CW].bitcast(F32R),
                                    start=(j == 0), stop=(j == DT - 1))
                p.s_mm.bump(inst)
                m_out.append(p.s_mm.n)
            p.m_rmm[o] = p.s_mm.n

        @blk.vector
        def _(v, o=o, m_out=m_out):
            for c in range(CH):
                cs = slice(c * CW, (c + 1) * CW)
                v.wait_ge(p.s_mm.h, m_out[c])
                inst = v.tensor_mul(rt[:, cs], p.pb[c][:, :], p.c2[:, cs])
                v.drain()
                p.s_ve.bump(inst)
                m_rev[(o, c)] = p.s_ve.n
        m_rtile = p.s_ve.n

        @blk.scalar
        def _(s, o=o, m_rtile=m_rtile):
            s.wait_ge(p.s_ve.h, m_rtile)
            inst = s.activation(out=rt[:, :], in_=rt[:, :], func=AF.Sigmoid)
            p.s_ac.bump(inst)
        m_sig = p.s_ac.n

        @blk.vector
        def _(v, o=o, m_sig=m_sig):
            v.wait_ge(p.s_ac.h, m_sig)
            inst = v.tensor_mul(p.rw[o][:, :], rt[:, :], p.rw[o][:, :])
            p.s_ve.bump(inst)
            for c in range(CH):
                m_rev[(o, c)] = p.s_ve.n
    p.m_rw = p.s_ve.n

    m_woev = {}
    S_wo = p.s_ve.n
    for o in range(DT):
        wbuf = woA if o % 2 == 0 else woB

        @blk.gpsimd
        def _(g, o=o, wbuf=wbuf):
            if o == 0:
                g.wait_ge(p.s_gp.h, p.m_state_in)
            if o >= 2:
                g.wait_ge(p.s_mm.h, p.m_womm[o - 2])
            colblock_load(p, g, "l0_wo", o, wbuf, p.s_wo0 if o % 2 == 0 else p.s_wo1)
        m_w = (p.s_wo0 if o % 2 == 0 else p.s_wo1).n
        if not hasattr(p, "m_womm"):
            p.m_womm = {}
        m_out = []

        @blk.tensor
        def _(t, o=o, wbuf=wbuf, m_w=m_w):
            if o == 0:
                t.wait_ge(p.s_ve.h, p.m_rw)
            t.wait_ge((p.s_wo0 if o % 2 == 0 else p.s_wo1).h, m_w)
            for c in range(CH):
                bank = p.pb[6 + c % 2]
                po, pc = (o, c - 2) if c >= 2 else (o - 1, c + 2)
                if po >= 0:
                    t.wait_ge(p.s_ve.h, S_wo + 4 * po + pc + 1)
                for j in range(DT):
                    inst = t.matmul(bank[:, :], wbuf[:, 128 * j:128 * (j + 1)],
                                    p.rw[j][:, c * CW:(c + 1) * CW],
                                    start=(j == 0), stop=(j == DT - 1))
                p.s_mm.bump(inst)
                m_out.append(p.s_mm.n)
            p.m_womm[o] = p.s_mm.n

        @blk.vector
        def _(v, o=o, m_out=m_out):
            mix = p.vecs[:, 16 + o:17 + o]
            for c in range(CH):
                cs = slice(c * CW, (c + 1) * CW)
                v.wait_ge(p.s_mm.h, m_out[c])
                inst = v.scalar_tensor_tensor(out=p.A[o][:, cs].bitcast(F32R),
                                              in0=p.pb[6 + c % 2][:, :],
                                              scalar=mix, op0=ALU.mult,
                                              in1=p.A[o][:, cs], op1=ALU.add)
                p.s_ve.bump(inst)
                m_woev[(o, c)] = p.s_ve.n
    p.m_xmid = p.s_ve.n
    tap(p, "xmid", [a[:, :] for a in p.A])


def emit_ffn_upgate(p, L):
    blk = p.block
    wq_set = [bf(p, 0, DT * CW // 2).bitcast(BF16), bf(p, 8, DT * CW // 2).bitcast(BF16)]
    stg = [bf(p, 16, CW), bf(p, 18, CW)]
    NB = I3 // CW
    m_start_ve = p.m_xmid if L == 0 else p.m_l1qkv
    k = 0
    m_st = {}
    sg_base = (p.s_sg0.n, p.s_sg1.n)
    for wi, wname in ((0, "fu"), (1, "fg")):
        dst = p.sp_up if wi == 0 else p.sp_ops
        for ob in range(NB):
            st_ = wq_set[k % 2]

            @blk.sync
            def _(sync, k=k, st_=st_, wname=wname, ob=ob):
                if k == 0:
                    sync.wait_ge(p.s_ve.h, m_start_ve)
                    sync.wait_ge(p.s_bn.h, p.m_bn[L])
                if k >= 2:
                    sync.wait_ge(p.s_mm.h, p.m_ugmm[k - 2])
                qsem = p.s_qA if k % 2 == 0 else p.s_qB
                for j in range(DT):
                    sync.dma_start(out=st_[:, CW * j:CW * (j + 1)],
                                   in_=p.lwT[(L, wname)][128 * j:128 * (j + 1),
                                                         ob * CW:(ob + 1) * CW]
                                   ).then_inc(qsem.h, 16)
                    qsem.n += 16
            m_tr = (p.s_qA if k % 2 == 0 else p.s_qB).n
            if not hasattr(p, "m_ugmm"):
                p.m_ugmm = {}

            for oi in range(CW // 128):
                row = ob * (CW // 128) + oi
                m_out = []

                @blk.tensor
                def _(t, k=k, st_=st_, oi=oi, m_tr=m_tr):
                    if k == 0 and oi == 0:
                        t.wait_ge(p.s_ve.h, p.m_xq)
                    t.wait_ge((p.s_qA if k % 2 == 0 else p.s_qB).h, m_tr)
                    for c in range(CH):
                        bank = p.pb[c]
                        gk = (k, oi - 1, c) if oi > 0 else (k - 1, CW // 128 - 1, c)
                        if gk in getattr(p, "m_ugev", {}):
                            t.wait_ge(p.s_ve.h, p.m_ugev[gk])
                        for j in range(DT):
                            inst = t.matmul(bank[:, :],
                                            st_[:, CW * j + 128 * oi:CW * j + 128 * (oi + 1)],
                                            p.xq[j][:, c * CW:(c + 1) * CW],
                                            start=(j == 0), stop=(j == DT - 1))
                        p.s_mm.bump(inst)
                        m_out.append(p.s_mm.n)
                if not hasattr(p, "m_ugev"):
                    p.m_ugev = {}

                @blk.vector
                def _(v, wi=wi, k=k, oi=oi, row=row, m_out=m_out):
                    col = 34 + wi
                    for c in range(CH):
                        idx = row * CH + c + wi * I3 // 128 * CH
                        sg = stg[idx % 2]
                        v.wait_ge(p.s_mm.h, m_out[c])
                        if idx >= 2:
                            pidx = idx - 2
                            sl = p.s_sg0 if pidx % 2 == 0 else p.s_sg1
                            v.wait_ge(sl.h, sg_base[pidx % 2] + (pidx // 2 + 1) * 16)
                        v.tensor_copy(sg, p.pb[c][:, :])
                        v.drain()
                        v.tensor_reduce(out=p.lmaxc[:, 36:37], in_=p.pb[c][:, :], axis=AX,
                                        op=ALU.max, apply_absolute_value=True)
                        v.drain()
                        if idx == 0 or (wi == 1 and row == 0 and c == 0):
                            v.tensor_copy(p.lmaxc[:, col:col + 1], p.lmaxc[:, 36:37])
                        else:
                            v.tensor_max(p.lmaxc[:, col:col + 1],
                                         p.lmaxc[:, col:col + 1], p.lmaxc[:, 36:37])
                        inst = v.drain()
                        p.s_ve.bump(inst)
                        p.m_ugev[(k, oi, c)] = p.s_ve.n

                @blk.scalar
                def _(s, wi=wi, k=k, oi=oi, row=row, dst=dst):
                    for c in range(CH):
                        idx = row * CH + c + wi * I3 // 128 * CH
                        sg = stg[idx % 2]
                        s.wait_ge(p.s_ve.h, p.m_ugev[(k, oi, c)])
                        sl = p.s_sg0 if idx % 2 == 0 else p.s_sg1
                        s.dma_start(out=dst[128 * row:128 * (row + 1),
                                            c * CW:(c + 1) * CW], in_=sg).then_inc(sl.h, 16)
                        sl.n += 16
                        m_st[idx] = sl.n
            p.m_ugmm[k] = p.s_mm.n
            k += 1
    p.m_ug_ve = p.s_ve.n
    p.m_ug_st = (p.s_sg0.n, p.s_sg1.n)
    tap_dram(p, f"D_up{L}", p.sp_up[:, :], [I3, T], F32)
    tap_dram(p, f"D_gp{L}", p.sp_ops[:, :], [I3, T], F32)


def emit_mlp_internal(p):
    blk, d = p.block, p.dram
    x2a, x2b = p.mu[:, 0:CW], p.mu[:, CW:2 * CW]

    S00 = p.s_mm.n
    for c in range(CH):
        cs = slice(c * CW, (c + 1) * CW)
        m_sq = []

        @blk.scalar
        def _(s, c=c, cs=cs):
            if c == 0:
                s.wait_ge(p.s_ve.h, p.m_xmid)
            for j in range(DT):
                m = c * DT + j
                if m >= 2:
                    s.wait_ge(p.s_mm.h, S00 + (m - 2) + 1)
                buf = x2a if j % 2 == 0 else x2b
                inst = s.activation(out=buf, in_=p.A[j][:, cs], func=AF.Square)
                p.s_ac.bump(inst)
                m_sq.append(p.s_ac.n)

        @blk.tensor
        def _(t, c=c, m_sq=m_sq):
            if c > 0:
                t.wait_ge(p.s_ve.h, p.m_c3ev)
            for j in range(DT):
                t.wait_ge(p.s_ac.h, m_sq[j])
                buf = x2a if j % 2 == 0 else x2b
                inst = t.matmul(p.pb[4][:, :], p.ones[:, :], buf,
                                start=(j == 0), stop=(j == DT - 1))
                p.s_mm.bump(inst)
        m_mm = p.s_mm.n

        @blk.vector
        def _(v, c=c, cs=cs, m_mm=m_mm):
            v.wait_ge(p.s_mm.h, m_mm)
            v.tensor_scalar(out=p.c2[:, cs], in0=p.pb[4][:, :], scalar1=INV_D,
                            scalar2=None, op0=ALU.mult)
            v.drain()
            v.tensor_scalar(out=p.c2[:, cs], in0=p.c2[:, cs], scalar1=1e-6,
                            scalar2=None, op0=ALU.add)
            v.drain()
            inst = v.reciprocal(p.c2[:, cs], p.c2[:, cs])
            p.s_ve.bump(inst)
        p.m_c3ev = p.s_ve.n
    p.m_c3 = p.s_ve.n

    wfcA, wfcB = p.wf[0], p.wf[1]
    tpr = T // 128
    nml_slots = 2 if T >= 2048 else 1

    def wml_tile(slot, j):
        base = p.rw[4 + (HT // tpr) * slot + j // tpr]
        return base[:, :].bitcast(BF16)[:, (j % tpr) * 128:(j % tpr) * 128 + 128]
    lstg = [p.mu[:, 2 * CW:3 * CW], p.mu[:, 3 * CW:4 * CW]]
    tstage = p.mu[:, 0:CW]

    def h2t(i):
        return p.xq[i // 4][:, (i % 4) * CW:((i % 4) + 1) * CW]

    for c in range(CH):
        cs = slice(c * CW, (c + 1) * CW)
        for h in range(HT):
            kk = c * HT + h
            wbuf = wfcA if kk % 2 == 0 else wfcB

            @blk.sync
            def _(sync, kk=kk, h=h, wbuf=wbuf):
                if kk == 0:
                    sync.wait_ge(p.s_ve.h, p.m_c3)
                if kk >= 2:
                    sync.wait_ge(p.s_mm.h, p.m_fcmm[kk - 2])
                colblock_load(p, sync, "l0_wfc", h, wbuf, p.s_wA if kk % 2 == 0 else p.s_wB)
            m_w = (p.s_wA if kk % 2 == 0 else p.s_wB).n
            if not hasattr(p, "m_fcmm"):
                p.m_fcmm = {}

            @blk.tensor
            def _(t, kk=kk, cs=cs, wbuf=wbuf, m_w=m_w):
                t.wait_ge((p.s_wA if kk % 2 == 0 else p.s_wB).h, m_w)
                if kk < 2:
                    t.wait_ge(p.s_ve.h, p.m_ug_ve)
                if kk >= 2:
                    t.wait_ge(p.s_ve.h, p.m_fcev.get(kk - 2, 0))
                bank = p.pb[kk % 2]
                for j in range(DT):
                    inst = t.matmul(bank[:, :], wbuf[:, 128 * j:128 * (j + 1)],
                                    p.A[j][:, cs].bitcast(F32R),
                                    start=(j == 0), stop=(j == DT - 1))
                p.s_mm.bump(inst)
                p.m_fcmm[kk] = p.s_mm.n
            m_mm2 = p.s_mm.n
            if not hasattr(p, "m_fcev"):
                p.m_fcev = {}

            @blk.vector
            def _(v, kk=kk, m_mm2=m_mm2):
                v.wait_ge(p.s_mm.h, m_mm2)
                if kk >= 2:
                    v.wait_ge(p.s_ac.h, p.m_h2s.get(kk - 2, 0))
                sg = lstg[kk % 2]
                v.tensor_copy(sg, p.pb[kk % 2][:, :])
                v.drain()
                inst = v.scalar_tensor_tensor(out=sg, in0=sg, scalar=0.5, op0=ALU.mult,
                                              in1=sg, op1=ALU.max)
                p.s_ve.bump(inst)
                p.m_fcev[kk] = p.s_ve.n
            if not hasattr(p, "m_h2s"):
                p.m_h2s = {}

            @blk.scalar
            def _(s, kk=kk, h=h):
                s.wait_ge(p.s_ve.h, p.m_fcev[kk])
                inst = s.activation(out=h2t(h), in_=lstg[kk % 2], func=AF.Square)
                p.s_ac.bump(inst)
                p.m_h2s[kk] = p.s_ac.n
        m_h2c = p.s_ac.n

        for o in range(DT):
            kk = c * DT + o
            slot = kk % nml_slots

            @blk.gpsimd
            def _(g, kk=kk, o=o, slot=slot):
                wsem = p.s_wm0 if slot == 0 else p.s_wm1
                if kk < nml_slots:
                    g.wait_ge(p.s_mm.h, p.m_ugmm[max(p.m_ugmm)])
                else:
                    g.wait_ge(p.s_mm.h, p.m_mlmm[kk - nml_slots])
                for j in range(HT):
                    g.dma_start(out=wml_tile(slot, j),
                                in_=p.dram["l0_wmlp"][128 * j:128 * (j + 1),
                                                      128 * o:128 * (o + 1)]).then_inc(wsem.h, 16)
                    wsem.n += 16
            m_w = (p.s_wm0 if slot == 0 else p.s_wm1).n
            if not hasattr(p, "m_mlmm"):
                p.m_mlmm = {}

            @blk.tensor
            def _(t, kk=kk, slot=slot, m_w=m_w, m_h2c=m_h2c):
                t.wait_ge((p.s_wm0 if slot == 0 else p.s_wm1).h, m_w)
                t.wait_ge(p.s_ac.h, m_h2c)
                if kk >= 2:
                    t.wait_ge(p.s_ve.h, p.m_mlev.get(kk - 2, 0))
                bank = p.pb[2 + kk % 2]
                for j in range(HT):
                    inst = t.matmul(bank[:, :], wml_tile(slot, j), h2t(j),
                                    start=(j == 0), stop=(j == HT - 1))
                p.s_mm.bump(inst)
                p.m_mlmm[kk] = p.s_mm.n
            m_mm3 = p.s_mm.n
            if not hasattr(p, "m_mlev"):
                p.m_mlev = {}

            @blk.vector
            def _(v, kk=kk, o=o, cs=cs, m_mm3=m_mm3):
                v.wait_ge(p.s_mm.h, m_mm3)
                mlps = p.vecs[:, 24 + o:25 + o]
                v.drain()
                v.tensor_mul(tstage, p.pb[2 + kk % 2][:, :], p.c2[:, cs])
                v.drain()
                inst = v.scalar_tensor_tensor(out=p.A[o][:, cs].bitcast(F32R),
                                              in0=tstage, scalar=mlps,
                                              op0=ALU.mult, in1=p.A[o][:, cs], op1=ALU.add)
                p.s_ve.bump(inst)
                p.m_mlev[kk] = p.s_ve.n
    p.m_s = p.s_ve.n
    tap(p, "s0", [a[:, :] for a in p.A])


def scale_scalar_chain(p, g):
    pass


def emit_ffn_tail(p, L):
    blk = p.block
    tin = bf(p, 0, T)
    ti = bf(p, 8, T).bitcast(I32)
    tsg = bf(p, 16, T)
    ti2 = bf(p, 24, T).bitcast(I32)
    t16 = [p.rw[0][:, :].bitcast(I16), p.rw[1][:, :].bitcast(I16)]
    tu16 = [p.rw[2][:, :].bitcast(U16), p.rw[3][:, :].bitcast(U16)]
    tg2 = bf(p, 32, T)

    ar_gather(p, [p.lmaxc[:, 34:35], p.lmaxc[:, 35:36]], [4, 5],
              wait_before=[(p.s_ve.h, p.m_ug_ve), (p.s_sg0.h, p.m_ug_st[0]),
                           (p.s_sg1.h, p.m_ug_st[1])])

    @blk.vector
    def _(v):
        v.wait_ge(p.s_ve.h, p.ar_ve)
        v.tensor_scalar(out=p.srow[0:1, 6:7], in0=p.srow[0:1, 4:5], scalar1=1.0 / QMAX,
                        op0=ALU.mult, scalar2=1e-12, op1=ALU.max)
        v.drain()
        v.tensor_scalar(out=p.srow[0:1, 7:8], in0=p.srow[0:1, 5:6], scalar1=1.0 / QMAX,
                        op0=ALU.mult, scalar2=1e-12, op1=ALU.max)
        v.drain()
        v.reciprocal(p.srow[0:1, 8:9], p.srow[0:1, 6:7])
        v.drain()
        inst = v.reciprocal(p.srow[0:1, 9:10], p.srow[0:1, 7:8])
        p.s_ve.bump(inst)
    bcast_cols(p, (6, 10), (2, 6))
    m_qsc2 = p.bc_guard

    m_store = {}
    for r in range(IT):
        @blk.sync
        def _(sync, r=r):
            if r == 0:
                sync.wait_ge(p.s_ve.h, m_qsc2)
            if r >= 1:
                sync.wait_ge(p.s_ve.h, p.m_g2.get(r - 1, 0))
            sync.dma_start(out=tin, in_=p.sp_ops[128 * r:128 * (r + 1), :]).then_inc(p.s_t.h, 16)
            p.s_t.n += 16
        m_in = p.s_t.n
        if not hasattr(p, "m_g2"):
            p.m_g2 = {}

        @blk.vector
        def _(v, r=r, m_in=m_in):
            v.wait_ge(p.s_t.h, m_in)
            if r >= 1:
                v.wait_ge(p.s_ts.h, m_store.get(r - 1, 0))
            v.tensor_scalar(out=ti, in0=tin, scalar1=p.qsc[:, 5:6], scalar2=None,
                            op0=ALU.mult)
            v.drain()
            inst = v.tensor_copy(t16[r % 2], ti)
            p.s_ve.bump(inst)
        m_i = p.s_ve.n

        @blk.scalar
        def _(s, r=r, m_i=m_i):
            s.wait_ge(p.s_ve.h, m_i)
            inst = s.activation(out=tsg, in_=ti, func=AF.Sigmoid, scale=p.qsc[:, 3:4])
            p.s_ac.bump(inst)
        m_sg = p.s_ac.n

        @blk.vector
        def _(v, r=r, m_sg=m_sg):
            v.wait_ge(p.s_ac.h, m_sg)
            v.tensor_scalar(out=ti2, in0=tsg, scalar1=65536.0, scalar2=None, op0=ALU.mult)
            v.drain()
            v.tensor_scalar(out=ti2, in0=ti2, scalar1=65535.0, scalar2=None, op0=ALU.min)
            v.drain()
            v.tensor_copy(tu16[r % 2], ti2)
            v.drain()
            v.tensor_tensor(out=tg2, in0=ti2, in1=ti, op=ALU.mult)
            v.drain()
            v.tensor_reduce(out=p.lmaxc[:, 37:38], in_=tg2, axis=AX, op=ALU.max,
                            apply_absolute_value=True)
            v.drain()
            if r == 0:
                v.tensor_copy(p.lmaxc[:, 38:39], p.lmaxc[:, 37:38])
            else:
                v.tensor_max(p.lmaxc[:, 38:39], p.lmaxc[:, 38:39], p.lmaxc[:, 37:38])
            inst = v.drain()
            p.s_ve.bump(inst)
            p.m_g2[r] = p.s_ve.n

        @blk.scalar
        def _(s, r=r):
            s.wait_ge(p.s_ve.h, p.m_g2[r])
            s.dma_start(out=p.sp_gi[128 * r:128 * (r + 1), :], in_=t16[r % 2]).then_inc(p.s_ts.h, 16)
            p.s_ts.n += 16
            s.dma_start(out=p.sp_su[128 * r:128 * (r + 1), :], in_=tu16[r % 2]).then_inc(p.s_ts.h, 16)
            p.s_ts.n += 16
            m_store[r] = p.s_ts.n
    m_gate_done = p.s_ts.n

    ar_gather(p, [p.lmaxc[:, 38:39]], [10],
              wait_before=[(p.s_ve.h, p.m_g2[IT - 1]), (p.s_ts.h, m_gate_done)])

    @blk.vector
    def _(v):
        v.wait_ge(p.s_ve.h, p.ar_ve)
        v.tensor_scalar(out=p.srow[0:1, 11:12], in0=p.srow[0:1, 10:11],
                        scalar1=p.srow[0:1, 7:8], scalar2=1.0 / 65536.0 / QMAX,
                        op0=ALU.mult, op1=ALU.mult)
        v.drain()
        v.tensor_scalar(out=p.srow[0:1, 11:12], in0=p.srow[0:1, 11:12],
                        scalar1=1e-12, scalar2=None, op0=ALU.max)
        v.drain()
        v.reciprocal(p.srow[0:1, 12:13], p.srow[0:1, 11:12])
        v.drain()
        inst = v.tensor_scalar(out=p.srow[0:1, 13:14], in0=p.srow[0:1, 12:13],
                               scalar1=p.srow[0:1, 7:8], scalar2=1.0 / 65536.0,
                               op0=ALU.mult, op1=ALU.mult)
        p.s_ve.bump(inst)
    bcast_cols(p, (11, 12), (6, 7))
    bcast_cols(p, (13, 14), (8, 9))
    m_qsc3 = p.bc_guard

    m_ops_st = {}
    for r in range(IT):
        @blk.sync
        def _(sync, r=r):
            if r == 0:
                sync.wait_ge(p.s_ve.h, m_qsc3)
            if r >= 1:
                sync.wait_ge(p.s_ve.h, p.m_ops.get(r - 1, 0))
            sync.dma_start(out=t16[0], in_=p.sp_gi[128 * r:128 * (r + 1), :]).then_inc(p.s_t.h, 16)
            sync.dma_start(out=tu16[0], in_=p.sp_su[128 * r:128 * (r + 1), :]).then_inc(p.s_t.h, 16)
            sync.dma_start(out=tin, in_=p.sp_up[128 * r:128 * (r + 1), :]).then_inc(p.s_t.h, 16)
            p.s_t.n += 48
        m_in = p.s_t.n
        if not hasattr(p, "m_ops"):
            p.m_ops = {}

        @blk.vector
        def _(v, r=r, m_in=m_in):
            v.wait_ge(p.s_t.h, m_in)
            if r >= 1:
                v.wait_ge(p.s_ts.h, m_ops_st.get(r - 1, 0))
            v.tensor_tensor(out=tg2, in0=tu16[0], in1=t16[0], op=ALU.mult)
            v.drain()
            v.tensor_scalar(out=ti2, in0=tg2, scalar1=p.qsc[:, 8:9], scalar2=None,
                            op0=ALU.mult)
            v.drain()
            v.tensor_scalar(out=ti, in0=tin, scalar1=p.qsc[:, 4:5], scalar2=None,
                            op0=ALU.mult)
            v.drain()
            v.tensor_tensor(out=tg2, in0=ti2, in1=ti, op=ALU.mult)
            v.drain()
            v.tensor_reduce(out=p.lmaxc[:, 37:38], in_=tg2, axis=AX, op=ALU.max,
                            apply_absolute_value=True)
            v.drain()
            if r == 0:
                v.tensor_copy(p.lmaxc[:, 39:40], p.lmaxc[:, 37:38])
            else:
                v.tensor_max(p.lmaxc[:, 39:40], p.lmaxc[:, 39:40], p.lmaxc[:, 37:38])
            inst = v.drain()
            p.s_ve.bump(inst)
            p.m_ops[r] = p.s_ve.n

        @blk.scalar
        def _(s, r=r):
            s.wait_ge(p.s_ve.h, p.m_ops[r])
            s.dma_start(out=p.sp_ops[128 * r:128 * (r + 1), :], in_=tg2).then_inc(p.s_ts.h, 16)
            p.s_ts.n += 16
            m_ops_st[r] = p.s_ts.n
    m_ops_done = p.s_ts.n

    ar_gather(p, [p.lmaxc[:, 39:40]], [14],
              wait_before=[(p.s_ve.h, p.m_ops[IT - 1]), (p.s_ts.h, m_ops_done)])

    @blk.vector
    def _(v):
        v.wait_ge(p.s_ve.h, p.ar_ve)
        v.tensor_scalar(out=p.srow[0:1, 15:16], in0=p.srow[0:1, 14:15],
                        scalar1=p.srow[0:1, 11:12], scalar2=p.srow[0:1, 6:7],
                        op0=ALU.mult, op1=ALU.mult)
        v.drain()
        v.tensor_scalar(out=p.srow[0:1, 15:16], in0=p.srow[0:1, 15:16],
                        scalar1=1.0 / QMAX, scalar2=1e-12, op0=ALU.mult, op1=ALU.max)
        v.drain()
        v.reciprocal(p.srow[0:1, 16:17], p.srow[0:1, 15:16])
        v.drain()
        inst = v.tensor_scalar(out=p.srow[0:1, 17:18], in0=p.srow[0:1, 16:17],
                               scalar1=p.srow[0:1, 11:12], scalar2=p.srow[0:1, 6:7],
                               op0=ALU.mult, op1=ALU.mult)
        p.s_ve.bump(inst)
    bcast_cols(p, (15, 16), (9, 10))
    bcast_cols(p, (17, 18), (10, 11))
    m_qsc4 = p.bc_guard

    m_obf_st = {}
    for r in range(IT):
        @blk.sync
        def _(sync, r=r):
            if r == 0:
                sync.wait_ge(p.s_ve.h, m_qsc4)
            if r >= 1:
                sync.wait_ge(p.s_ve.h, p.m_obf.get(r - 1, 0))
            sync.dma_start(out=tg2, in_=p.sp_ops[128 * r:128 * (r + 1), :]).then_inc(p.s_t.h, 16)
            p.s_t.n += 16
        m_in = p.s_t.n
        if not hasattr(p, "m_obf"):
            p.m_obf = {}

        @blk.vector
        def _(v, r=r, m_in=m_in):
            v.wait_ge(p.s_t.h, m_in)
            if r >= 1:
                v.wait_ge(p.s_ts.h, m_obf_st.get(r - 1, 0))
            v.tensor_scalar(out=ti, in0=tg2, scalar1=p.qsc[:, 10:11], scalar2=None,
                            op0=ALU.mult)
            v.drain()
            inst = v.tensor_scalar(out=tin.bitcast(BF16)[:, 0:T], in0=ti,
                                   scalar1=p.qsc[:, 9:10], scalar2=None, op0=ALU.mult)
            p.s_ve.bump(inst)
            p.m_obf[r] = p.s_ve.n

        @blk.scalar
        def _(s, r=r):
            s.wait_ge(p.s_ve.h, p.m_obf[r])
            s.dma_start(out=p.sp_obf[128 * r:128 * (r + 1), :],
                        in_=tin.bitcast(BF16)[:, 0:T]).then_inc(p.s_ts.h, 16)
            p.s_ts.n += 16
            m_obf_st[r] = p.s_ts.n
    p.m_obf_done = p.s_ts.n
    tap_dram(p, f"D_obf{L}", p.sp_obf[:, :], [I3, T], BF16)


def emit_down_combine(p, L):
    blk, d = p.block, p.dram
    xsrc = d["xT"] if L == 0 else p.x1T
    dst = (p.x1T if L == 0 else p.out) if not (L0_ONLY and L == 0) else p.out
    wdt = [bf(p, 0, 512).bitcast(BF16), bf(p, 2, 512).bitcast(BF16)]
    obt = [bf(p, 4, CW // 2).bitcast(BF16), bf(p, 5, CW // 2).bitcast(BF16)]
    gst = [bf(p, 6, CW), bf(p, 8, CW)]
    cst = [bf(p, 10, CW), bf(p, 12, CW)]
    xst = [bf(p, 14, CW), bf(p, 16, CW)]
    wsl = [p.s_wd0, p.s_wd1]
    m_y = {}
    m_mmj = {}
    m_ev = {}
    for c in range(CH):
        cs = slice(c * CW, (c + 1) * CW)
        for j in range(IT):
            sl = j % 2

            @blk.sync
            def _(sync, c=c, j=j, sl=sl):
                if c == 0 and j == 0:
                    sync.wait_ge(p.s_ts.h, p.m_obf_done)
                if (c, j) != (0, 0) and (c * IT + j) >= 2:
                    kk = c * IT + j - 2
                    sync.wait_ge(p.s_mm.h, m_mmj[(kk // IT, kk % IT)])
                sync.dma_start(out=wdt[sl][:, :],
                               in_=p.lwT[(L, "fd")][128 * j:128 * (j + 1), :]
                               ).then_inc(wsl[sl].h, 16)
                wsl[sl].n += 16
                sync.dma_start(out=obt[sl][:, :], in_=p.sp_obf[128 * j:128 * (j + 1), cs]).then_inc(wsl[sl].h, 16)
                wsl[sl].n += 16
            m_w = wsl[sl].n

            @blk.tensor
            def _(t, c=c, j=j, sl=sl, m_w=m_w):
                t.wait_ge(wsl[sl].h, m_w)
                if c > 0 and j == 0:
                    for o in range(DT):
                        t.wait_ge(p.s_ac.h, m_ev[(c - 1, o)])
                for o in range(DT):
                    inst = t.matmul(p.pb[o][:, :], wdt[sl][:, 128 * o:128 * (o + 1)],
                                    obt[sl][:, :], start=(j == 0), stop=(j == IT - 1))
                p.s_mm.bump(inst)
                m_mmj[(c, j)] = p.s_mm.n
        m_last = p.s_mm.n

        for o in range(DT):
            kk = c * DT + o

            @blk.sync
            def _(sync, kk=kk, o=o, cs=cs):
                xcs = p.s_xc0 if kk % 2 == 0 else p.s_xc1
                if kk >= 2:
                    sync.wait_ge(p.s_ve.h, p.m_cmb.get(kk - 2, 0))
                sync.dma_start(out=xst[kk % 2], in_=xsrc[128 * o:128 * (o + 1), cs]).then_inc(xcs.h, 16)
                xcs.n += 16
            m_x = (p.s_xc0 if kk % 2 == 0 else p.s_xc1).n
            if not hasattr(p, "m_cmb"):
                p.m_cmb = {}

            @blk.scalar
            def _(s, kk=kk, o=o, c=c, m_last=m_last):
                s.wait_ge(p.s_mm.h, m_last)
                if kk >= 2:
                    s.wait_ge(p.s_ve.h, p.m_cmb.get(kk - 2, 0))
                    if SPLIT1 and L == 1:
                        ysp = p.s_y0 if kk % 2 == 0 else p.s_y1
                        s.wait_ge(ysp.h, m_y.get(kk - 2, 0))
                inst = s.activation(out=gst[kk % 2], in_=p.pb[o][:, :], func=AF.Silu)
                p.s_ac.bump(inst)
                m_ev[(c, o)] = p.s_ac.n
            m_si = p.s_ac.n

            @blk.vector
            def _(v, kk=kk, o=o, cs=cs, m_si=m_si, m_x=m_x):
                v.wait_ge(p.s_ac.h, m_si)
                v.wait_ge((p.s_xc0 if kk % 2 == 0 else p.s_xc1).h, m_x)
                if kk >= 2:
                    v.wait_ge((p.s_y0 if kk % 2 == 0 else p.s_y1).h, m_y.get(kk - 2, 0))
                v.drain()
                v.tensor_mul(cst[kk % 2], gst[kk % 2], p.A[o][:, cs])
                v.drain()
                inst = v.tensor_add(cst[kk % 2], cst[kk % 2], xst[kk % 2])
                p.s_ve.bump(inst)
                p.m_cmb[kk] = p.s_ve.n
            m_cb = p.s_ve.n

            @blk.sync
            def _(sync, kk=kk, o=o, cs=cs, m_cb=m_cb, dst=dst):
                sync.wait_ge(p.s_ve.h, m_cb)
                ys = p.s_y0 if kk % 2 == 0 else p.s_y1
                src_t = gst[kk % 2] if (SPLIT1 and L == 1) else cst[kk % 2]
                dst_t = p.dram["g_out"] if (SPLIT1 and L == 1) else dst
                sync.dma_start(out=dst_t[128 * o:128 * (o + 1), cs],
                               in_=src_t).then_inc(ys.h, 16)
                ys.n += 16
                m_y[kk] = ys.n
    p.m_y_done = (p.s_y0.n, p.s_y1.n)
    for nm in ("m_cmb", "m_ugmm", "m_ugev", "m_g2",
               "m_ops", "m_obf", "m_fcmm", "m_fcev", "m_h2s", "m_mlmm", "m_mlev",
               "m_kvmm", "m_rmm", "m_womm"):
        if hasattr(p, nm):
            delattr(p, nm)


def emit_final_wait(p):
    @p.block.sync
    def _(sync):
        sync.wait_ge(p.s_y0.h, p.m_y_done[0])
        sync.wait_ge(p.s_y1.h, p.m_y_done[1])
        sync.wait_ge(p.s_ve.h, p.s_ve.n)
        sync.dma_start(out=p.dbg[0:1, 0:32], in_=p.srow[0:1, 0:32]).then_inc(p.s_tap.h, 16)
        sync.dma_start(out=p.dbg[0:1, 32:48], in_=p.qsc[0:1, 0:16]).then_inc(p.s_tap.h, 16)
        sync.dma_start(out=p.dbg[0:1, 48:88], in_=p.lmaxc[0:1, 0:40]).then_inc(p.s_tap.h, 16)
        sync.dma_start(out=p.dbg[0:1, 96:152], in_=p.vecs[0:1, 0:56]).then_inc(p.s_tap.h, 16)
        sync.dma_start(out=p.dbg[0:1, 160:192], in_=p.xq[0][0:1, 0:64].bitcast(F32)).then_inc(p.s_tap.h, 16)
        sync.dma_start(out=p.dbg[0:1, 192:200], in_=p.qsc[64:65, 0:8]).then_inc(p.s_tap.h, 16)
        sync.dma_start(out=p.dbg[0:1, 200:216], in_=p.A[0][0:1, 0:16]).then_inc(p.s_tap.h, 16)
        p.s_tap.n += 112
        for nm in ("s_ld", "s_st", "s_gd", "s_boot", "s_x", "s_wA", "s_wB",
                   "s_qA", "s_qB", "s_t", "s_ob", "s_wd0", "s_wd1", "s_xc0",
                   "s_xc1", "s_sg0", "s_sg1", "s_ts", "s_ar", "s_wo0", "s_wo1",
                   "s_wm0", "s_wm1", "s_dqg", "s_z", "s_tap", "s_dq"):
            sm = getattr(p, nm)
            if sm.n:
                sync.wait_ge(sm.h, sm.n)


def emit_l1_qkv(p):
    blk, d = p.block, p.dram
    S_ac0 = p.s_ac.n
    stg = [p.rw[4][:, 0:CW], p.rw[4][:, CW:2 * CW],
           p.rw[5][:, 0:CW], p.rw[5][:, CW:2 * CW]]
    m_ev = {}
    m_st = {}
    p.m_qkmm = {}
    ki = 0
    for wname, oname, qscale in (("l1_wq", "q_out", None), ("l1_wk", "k_out", None),
                                 ("l1_wv", "v_out", None)):
        for o in range(DT):
            wbuf = p.wf[ki % 2]

            @blk.sync
            def _(sync, ki=ki, wname=wname, wbuf=wbuf):
                if ki == 0:
                    sync.wait_ge(p.s_ve.h, p.m_xq)
                if ki >= 2:
                    sync.wait_ge(p.s_mm.h, p.m_qkmm[ki - 2])
                colblock_load(p, sync, wname, o, wbuf,
                              p.s_wA if ki % 2 == 0 else p.s_wB)
            m_w = (p.s_wA if ki % 2 == 0 else p.s_wB).n

            m_out = []

            @blk.tensor
            def _(t, ki=ki, wbuf=wbuf, m_w=m_w):
                t.wait_ge((p.s_wA if ki % 2 == 0 else p.s_wB).h, m_w)
                if ki == 0:
                    t.wait_ge(p.s_ac.h, S_ac0)
                for c in range(CH):
                    gk = (ki - 1) * CH + c
                    if gk >= 0 and gk in m_ev:
                        t.wait_ge(p.s_ve.h, m_ev[gk])
                    for j in range(DT):
                        inst = t.matmul(p.pb[c][:, :], wbuf[:, 128 * j:128 * (j + 1)],
                                        p.A[j][:, c * CW:(c + 1) * CW].bitcast(F32R),
                                        start=(j == 0), stop=(j == DT - 1))
                    p.s_mm.bump(inst)
                    m_out.append(p.s_mm.n)
                p.m_qkmm[ki] = p.s_mm.n

            @blk.vector
            def _(v, ki=ki, m_out=m_out, qscale=qscale):
                for c in range(CH):
                    g = ki * CH + c
                    sl = stg[c]
                    v.wait_ge(p.s_mm.h, m_out[c])
                    if g - 4 in m_st:
                        sem = p.s_wo0 if (g - 4) % 2 == 0 else p.s_wo1
                        v.wait_ge(sem.h, m_st[g - 4])
                    if qscale is None:
                        inst = v.tensor_copy(sl, p.pb[c][:, :])
                    else:
                        inst = v.tensor_scalar(out=sl, in0=p.pb[c][:, :],
                                               scalar1=qscale, scalar2=None, op0=ALU.mult)
                    v.drain()
                    p.s_ve.bump(inst)
                    m_ev[g] = p.s_ve.n

            @blk.scalar
            def _(s, ki=ki, o=o, oname=oname):
                for c in range(CH):
                    g = ki * CH + c
                    sl = stg[c]
                    s.wait_ge(p.s_ve.h, m_ev[g])
                    sem = p.s_wo0 if g % 2 == 0 else p.s_wo1
                    s.dma_start(out=d[oname].bitcast(BF16)[128 * o:128 * (o + 1),
                                                           c * CW:(c + 1) * CW],
                                in_=sl).then_inc(sem.h, 16)
                    sem.n += 16
                    m_st[g] = sem.n
            ki += 1
    p.m_l1qkv = p.s_ve.n


def emit_l1_mixer(p):
    raise NotImplementedError


def emit_l1_attn_all(p):
    raise NotImplementedError


def emit_l1_wo(p):
    raise NotImplementedError


def make_ident():
    import ml_dtypes
    return np.eye(128, dtype=ml_dtypes.bfloat16)


def make_masks():
    import ml_dtypes
    return np.zeros((128, 2048), ml_dtypes.bfloat16)


def host_inputs(inputs):
    xT = np.ascontiguousarray(np.asarray(inputs["L0_attn_x"]).transpose(0, 2, 1))

    def wT(a):
        return np.ascontiguousarray(np.asarray(a).T)

    vec0 = np.concatenate([
        np.asarray(inputs["L0_attn_time_decay"]).reshape(-1),
        np.asarray(inputs["L0_attn_time_first"]).reshape(-1),
        np.asarray(inputs["L0_attn_mixer_scale"]).reshape(-1),
        np.asarray(inputs["L0_attn_mlp_scale"]).reshape(-1),
    ]).reshape(1, -1).astype(np.float32)
    in_maps = []
    for c in range(8):
        b, h = c // 2, c % 2
        par = c % 2
        im = {
            "xT": np.ascontiguousarray(xT[b][:, h * T:(h + 1) * T]),
            "l0_wk": wT(inputs["L0_attn_Wk"]), "l0_wv": wT(inputs["L0_attn_Wv"]),
            "l0_wr": wT(inputs["L0_attn_Wr"]), "l0_wo": wT(inputs["L0_attn_Wo"]),
            "l0_wfc": wT(inputs["L0_attn_Wfc"]), "l0_wmlp": wT(inputs["L0_attn_Wmlp"]),
            "l0_vec": vec0,
            "l1_wq": wT(inputs["L1_attn_Wq"]), "l1_wk": wT(inputs["L1_attn_Wk"]),
            "l1_wv": wT(inputs["L1_attn_Wv"]), "l1_wo": wT(inputs["L1_attn_Wo"]),
            "c_ident": make_ident(), "c_ident32": np.eye(128, dtype=np.float32), "c_masks": make_masks(),
        }
        for L in (0, 1):
            fu = np.asarray(inputs[f"L{L}_ffn_w_up"])
            fg = np.asarray(inputs[f"L{L}_ffn_w_gate"])
            fd = np.asarray(inputs[f"L{L}_ffn_w_down"])
            im[f"l{L}_fu_my"] = np.ascontiguousarray(fu[par * (I3 // 2):(par + 1) * (I3 // 2)])
            im[f"l{L}_fg_my"] = np.ascontiguousarray(fg[par * (I3 // 2):(par + 1) * (I3 // 2)])
            im[f"l{L}_fd_my"] = np.ascontiguousarray(fd[par * (D // 2):(par + 1) * (D // 2)])
        in_maps.append(im)
    return in_maps


def build_attn():
    nc = bass.Bass("TRN2", target_bir_lowering=False, debug=False)
    qh = nc.dram_tensor("qh", [512, T], F32, kind="ExternalInput").ap()
    kh = nc.dram_tensor("kh", [512, T], F32, kind="ExternalInput").ap()
    vh = nc.dram_tensor("vh", [512, T], F32, kind="ExternalInput").ap()
    amask = nc.dram_tensor("amask", [S, S // 2], F32, kind="ExternalInput").ap()
    cid = nc.dram_tensor("cid", [128, 64], F32, kind="ExternalInput").ap()
    oT = nc.dram_tensor("oT", [512, T], F32, kind="ExternalOutput").ap()
    es = contextlib.ExitStack()
    sb = lambda name, shape, dt: es.enter_context(nc.sbuf_tensor(name, shape, dt))
    sem = lambda name: es.enter_context(nc.semaphore(name))
    with es:
        qs = [sb(f"qs{t}", [128, 2 * T], BF16) for t in range(4)]
        ks = [sb(f"ks{t}", [128, 2 * T], BF16) for t in range(4)]
        vs = [sb(f"vs{t}", [128, 2 * T], BF16) for t in range(4)]
        VT8 = [sb(f"vt{m}", [128, T], BF16) for m in range(8)]
        P = sb("P", [128, 2 * T], BF16)
        PTs = [sb(f"pt{i}", [128, 128], BF16) for i in range(2)]
        chx = [sb(f"chx{i}", [128, CW], F32) for i in range(2)]
        mk = [sb(f"mk{i}", [128, 2 * T], BF16) for i in range(2)]
        lcol = sb("lcol", [128, 12], F32)
        obb = sb("obb", [128, 64], BF16)
        oTt = sb("oTt", [128, 128], BF16)
        idb = sb("idb", [128, 128], BF16)
        pb = [es.enter_context(nc.psum_tensor(f"pb{i}", [128, CW], F32)) for i in range(8)]
        ld = sem("ld"); mms = sem("mms"); ves = sem("ves"); acs = sem("acs")
        sts = sem("sts"); mkl = sem("mkl")
        nld = [0]; nmm = [0]; nve = [0]; nac = [0]; nst = [0]; nmk = [0]
        blk = es.enter_context(nc.Block())

        @blk.sync
        def _(sync):
            for t in range(4):
                for nm, src in ((qs, qh), (ks, kh), (vs, vh)):
                    sync.dma_start(out=nm[t][:, :].bitcast(F32),
                                   in_=src[128 * t:128 * (t + 1), :]).then_inc(ld, 16)
                    nld[0] += 16
            sync.dma_start(out=idb[:, :].bitcast(F32), in_=cid[:, :]).then_inc(ld, 16)
            nld[0] += 16
        NLD = nld[0]

        mvt = {}
        for m in range(8):
            mi, hh = m // 2, m % 2
            for kt in range(32):
                @blk.tensor
                def _(t, m=m, mi=mi, hh=hh, kt=kt):
                    if m == 0 and kt == 0:
                        t.wait_ge(ld, NLD)
                    prev = (m, kt - 2) if kt >= 2 else (m - 1, 30 + kt)
                    if prev in mvt:
                        t.wait_ge(ves, mvt[prev])
                    inst = t.matmul(pb[kt % 2][:, 0:64],
                                    vs[mi][64 * hh:64 * hh + 64, 128 * kt:128 * (kt + 1)],
                                    idb[64 * hh:64 * hh + 64, 64 * hh:64 * hh + 64],
                                    start=True, stop=True)
                    inst.then_inc(mms, 1)
                    nmm[0] += 1
                    mvt[(m, kt, "mm")] = nmm[0]

                @blk.vector
                def _(v, m=m, kt=kt):
                    v.wait_ge(mms, mvt[(m, kt, "mm")])
                    inst = v.tensor_copy(VT8[m][:, 64 * kt:64 * (kt + 1)], pb[kt % 2][:, 0:64])
                    v.drain()
                    inst.then_inc(ves, 1)
                    nve[0] += 1
                    mvt[(m, kt)] = nve[0]
        V_vt_done = nve[0]

        mk_used = {}
        for g in range(32):
            @blk.sync
            def _(sync, g=g):
                if g >= 2:
                    sync.wait_ge(ves, mk_used[g - 2])
                sync.dma_start(out=mk[g % 2][:, :].bitcast(F32),
                               in_=amask[128 * g:128 * (g + 1), :]).then_inc(mkl, 16)
                nmk[0] += 16
            NMK = nmk[0]
            nch = (g + 4) // 4
            for m in range(8):
                mi, hh = m // 2, m % 2
                ve_prev = nve[0]
                ac_prev = nac[0]
                mme = {}

                for ch in range(nch):
                    @blk.tensor
                    def _(t, g=g, mi=mi, hh=hh, ch=ch, ve_prev=ve_prev):
                        if g == 0 and m == 0 and ch == 0:
                            t.wait_ge(ld, NLD)
                        if ch == 0:
                            t.wait_ge(ves, ve_prev)
                        inst = t.matmul(pb[ch][:, :],
                                        qs[mi][64 * hh:64 * hh + 64, 128 * g:128 * (g + 1)],
                                        ks[mi][64 * hh:64 * hh + 64, CW * ch:CW * (ch + 1)],
                                        start=True, stop=True)
                        inst.then_inc(mms, 1)
                        nmm[0] += 1
                        mme[("s", ch)] = nmm[0]

                    @blk.vector
                    def _(v, g=g, ch=ch, NMK=NMK):
                        if ch == 0:
                            v.wait_ge(mkl, NMK)
                        v.wait_ge(mms, mme[("s", ch)])
                        if ch >= 2:
                            v.wait_ge(acs, mme[("e", ch - 2)])
                        inst = v.tensor_tensor(out=chx[ch % 2][:, :], in0=pb[ch][:, :],
                                               in1=mk[g % 2][:, CW * ch:CW * (ch + 1)],
                                               op=ALU.add)
                        v.drain()
                        inst.then_inc(ves, 1)
                        nve[0] += 1
                        mme[("x", ch)] = nve[0]

                    @blk.scalar
                    def _(s, ch=ch):
                        s.wait_ge(ves, mme[("x", ch)])
                        inst = s.activation(out=P[:, CW * ch:CW * (ch + 1)], in_=chx[ch % 2][:, :],
                                            func=AF.Exp, scale=0.125,
                                            accum_out=lcol[:, ch:ch + 1])
                        inst.then_inc(acs, 1)
                        nac[0] += 1
                        mme[("e", ch)] = nac[0]

                @blk.vector
                def _(v, g=g, nch=nch):
                    v.wait_ge(acs, mme[("e", nch - 1)])
                    v.tensor_reduce(out=lcol[:, 8:9], in_=lcol[:, 0:nch], axis=AX,
                                    op=ALU.add)
                    v.drain()
                    inst = v.reciprocal(lcol[:, 9:10], lcol[:, 8:9])
                    v.drain()
                    inst.then_inc(ves, 1)
                    nve[0] += 1
                    mme["rl"] = nve[0]

                for kt in range(g + 1):
                    @blk.tensor
                    def _(t, g=g, kt=kt, nch=nch):
                        if kt == 0:
                            t.wait_ge(acs, mme[("e", nch - 1)])
                        inst = t.matmul(pb[4 + kt % 2][:, 0:128],
                                        P[:, 128 * kt:128 * (kt + 1)], idb[:, :],
                                        start=True, stop=True)
                        inst.then_inc(mms, 1)
                        nmm[0] += 1
                        mme[("t", kt)] = nmm[0]

                    @blk.vector
                    def _(v, kt=kt):
                        v.wait_ge(mms, mme[("t", kt)])
                        if kt >= 2:
                            v.wait_ge(mms, mme[("a", kt - 2)])
                        inst = v.tensor_copy(PTs[kt % 2][:, :], pb[4 + kt % 2][:, 0:128])
                        v.drain()
                        inst.then_inc(ves, 1)
                        nve[0] += 1
                        mme[("pt", kt)] = nve[0]

                    @blk.tensor
                    def _(t, g=g, m=m, kt=kt):
                        t.wait_ge(ves, mme[("pt", kt)])
                        inst = t.matmul(pb[6][:, 0:64], PTs[kt % 2][:, :],
                                        VT8[m][:, 64 * kt:64 * (kt + 1)],
                                        start=(kt == 0), stop=(kt == g))
                        inst.then_inc(mms, 1)
                        nmm[0] += 1
                        mme[("a", kt)] = nmm[0]

                @blk.vector
                def _(v, g=g, m=m):
                    v.wait_ge(mms, mme[("a", g)])
                    v.wait_ge(ves, mme["rl"])
                    inst = v.tensor_scalar(out=obb[:, :], in0=pb[6][:, 0:64],
                                           scalar1=lcol[:, 9:10], scalar2=None, op0=ALU.mult)
                    v.drain()
                    inst.then_inc(ves, 1)
                    nve[0] += 1
                    mme["ob"] = nve[0]

                @blk.tensor
                def _(t):
                    t.wait_ge(ves, mme["ob"])
                    inst = t.matmul(pb[7][0:64, 0:128], obb[:, :], idb[:, :],
                                    start=True, stop=True)
                    inst.then_inc(mms, 1)
                    nmm[0] += 1
                    mme["obT"] = nmm[0]

                @blk.vector
                def _(v):
                    v.wait_ge(mms, mme["obT"])
                    inst = v.tensor_copy(oTt[0:64, :], pb[7][0:64, 0:128])
                    v.drain()
                    inst.then_inc(ves, 1)
                    nve[0] += 1
                    mme["oTt"] = nve[0]

                @blk.sync
                def _(sync, g=g, m=m):
                    sync.wait_ge(ves, mme["oTt"])
                    sync.dma_start(out=oT.bitcast(BF16)[64 * m:64 * (m + 1),
                                                        128 * g:128 * (g + 1)],
                                   in_=oTt[0:64, :]).then_inc(sts, 16)
                    nst[0] += 16
            mk_used[g] = nve[0]

        @blk.sync
        def _(sync):
            sync.wait_ge(sts, nst[0])
    return nc


def build_wo3():
    nc = bass.Bass("TRN2", target_bir_lowering=False, debug=False)
    o_in = nc.dram_tensor("o_in", [D, T // 2], F32, kind="ExternalInput").ap()
    x1c = nc.dram_tensor("x1c", [D, T], F32, kind="ExternalInput").ap()
    gc = nc.dram_tensor("gc", [D, T], F32, kind="ExternalInput").ap()
    wot = nc.dram_tensor("wot", [D, D], F32, kind="ExternalInput").ap()
    out3 = nc.dram_tensor("out3", [D, T], F32, kind="ExternalOutput").ap()
    es = contextlib.ExitStack()
    sb = lambda name, shape, dt: es.enter_context(nc.sbuf_tensor(name, shape, dt))
    sem = lambda name: es.enter_context(nc.semaphore(name))
    with es:
        ot = [sb(f"ot{j}", [128, T], BF16) for j in range(DT)]
        wob = [sb(f"wob{i}", [128, D], BF16) for i in range(2)]
        xsl = [sb(f"xsl{i}", [128, CW], F32) for i in range(4)]
        cst = [sb(f"cst{i}", [128, CW], F32) for i in range(2)]
        pb = [es.enter_context(nc.psum_tensor(f"pb{i}", [128, CW], F32)) for i in range(4)]
        ld = sem("ld"); wl = sem("wl"); mms = sem("mms"); ves = sem("ves"); sts = sem("sts")
        nld = [0]; nwl = [0]; nmm = [0]; nve = [0]; nst = [0]
        blk = es.enter_context(nc.Block())

        @blk.sync
        def _(sync):
            for j in range(DT):
                sync.dma_start(out=ot[j][:, :].bitcast(F32),
                               in_=o_in[128 * j:128 * (j + 1), :]).then_inc(ld, 16)
                nld[0] += 16
        NLD = nld[0]
        mmo = {}
        mcb = {}
        for o in range(DT):
            @blk.gpsimd
            def _(g2, o=o):
                if o >= 2:
                    g2.wait_ge(mms, mmo[o - 2])
                for j in range(DT):
                    g2.dma_start(out=wob[o % 2][:, 128 * j:128 * (j + 1)],
                                 in_=wot[128 * j:128 * (j + 1),
                                         128 * o:128 * (o + 1)]).then_inc(wl, 16)
                    nwl[0] += 16
            NWL = nwl[0]

            @blk.tensor
            def _(t, o=o, NWL=NWL):
                if o == 0:
                    t.wait_ge(ld, NLD)
                t.wait_ge(wl, NWL)
                for c in range(CH):
                    kk = o * CH + c
                    if kk - 4 in mcb:
                        t.wait_ge(ves, mcb[kk - 4])
                    for j in range(DT):
                        inst = t.matmul(pb[c][:, :], wob[o % 2][:, 128 * j:128 * (j + 1)],
                                        ot[j][:, CW * c:CW * (c + 1)],
                                        start=(j == 0), stop=(j == DT - 1))
                    inst.then_inc(mms, 1)
                    nmm[0] += 1
                    mmo[(o, c)] = nmm[0]
                mmo[o] = nmm[0]

            for c in range(CH):
                kk = o * CH + c

                @blk.sync
                def _(sync, o=o, c=c, kk=kk):
                    if kk - 2 in mcb:
                        sync.wait_ge(ves, mcb[kk - 2])
                    sync.dma_start(out=xsl[2 * (kk % 2)][:, :],
                                   in_=x1c[128 * o:128 * (o + 1), CW * c:CW * (c + 1)]).then_inc(ld, 16)
                    sync.dma_start(out=xsl[2 * (kk % 2) + 1][:, :],
                                   in_=gc[128 * o:128 * (o + 1), CW * c:CW * (c + 1)]).then_inc(ld, 16)
                    nld[0] += 32
                NL2 = nld[0]

                @blk.vector
                def _(v, o=o, c=c, kk=kk, NL2=NL2):
                    v.wait_ge(mms, mmo[(o, c)])
                    v.wait_ge(ld, NL2)
                    if kk - 2 in mcb:
                        v.wait_ge(sts, 16 * (kk - 1))
                    v.tensor_mul(cst[kk % 2][:, :], pb[c][:, :], xsl[2 * (kk % 2) + 1][:, :])
                    v.drain()
                    inst = v.tensor_add(cst[kk % 2][:, :], cst[kk % 2][:, :], xsl[2 * (kk % 2)][:, :])
                    v.drain()
                    inst.then_inc(ves, 1)
                    nve[0] += 1
                    mcb[kk] = nve[0]

                @blk.sync
                def _(sync, o=o, c=c, kk=kk):
                    sync.wait_ge(ves, mcb[kk])
                    sync.dma_start(out=out3[128 * o:128 * (o + 1), CW * c:CW * (c + 1)],
                                   in_=cst[kk % 2][:, :]).then_inc(sts, 16)
                    nst[0] += 16

        @blk.sync
        def _(sync):
            sync.wait_ge(sts, nst[0])
    return nc


def attn_masks():
    import ml_dtypes
    i = np.arange(S)[:, None]
    j = np.arange(S)[None, :]
    causal = j <= i
    out = []
    for ty in (0, 1):
        if ty == 0:
            mm = causal & ((i - j < 64) | (j % 64 == i % 64))
        else:
            mm = causal & ((j // 64 == i // 64) | (j % 64 >= 63))
        mk = np.where(mm, np.float32(-160.0), np.float32(-1e38)).astype(ml_dtypes.bfloat16)
        out.append(np.ascontiguousarray(mk.view(np.uint16).reshape(S, S // 2, 2)
                                        .view(np.uint32).reshape(S, S // 2).view(np.float32)))
    return out


def kernel(**inputs):
    import ml_dtypes
    nc, p = build_program(TAPS)
    in_maps = host_inputs(inputs)
    res = run_bass_kernel_spmd(nc, in_maps, core_ids=list(range(8)))
    r1 = [res.results[c] for c in range(8)]
    idb = np.eye(128, dtype=ml_dtypes.bfloat16)
    cid = idb.view(np.uint16).reshape(128, 64, 2).view(np.uint32).reshape(128, 64).view(np.float32)
    masks = attn_masks()
    nc2 = build_attn()
    im2 = []
    for c2 in range(8):
        b, ty = c2 // 2, c2 % 2
        im = {"cid": cid, "amask": masks[ty]}
        for nm, key in (("qh", "q_out"), ("kh", "k_out"), ("vh", "v_out")):
            h0 = np.asarray(r1[2 * b][key])[512 * ty:512 * (ty + 1), :]
            h1 = np.asarray(r1[2 * b + 1][key])[512 * ty:512 * (ty + 1), :]
            im[nm] = np.ascontiguousarray(np.concatenate([h0, h1], axis=1))
        im2.append(im)
    res2 = run_bass_kernel_spmd(nc2, im2, core_ids=list(range(8)))
    nc3 = build_wo3()
    wot = np.ascontiguousarray(np.asarray(inputs["L1_attn_Wo"]).T)
    im3 = []
    for c in range(8):
        b, h = c // 2, c % 2
        o0 = np.asarray(res2.results[2 * b][ "oT"])[:, 1024 * h:1024 * (h + 1)]
        o1 = np.asarray(res2.results[2 * b + 1]["oT"])[:, 1024 * h:1024 * (h + 1)]
        im3.append({
            "o_in": np.ascontiguousarray(np.concatenate([o0, o1], axis=0)),
            "x1c": np.ascontiguousarray(np.asarray(r1[c]["x1T"])),
            "gc": np.ascontiguousarray(np.asarray(r1[c]["g_out"])),
            "wot": wot,
        })
    res3 = run_bass_kernel_spmd(nc3, im3, core_ids=list(range(8)))
    out = np.empty((B, S, D), np.float32)
    for c in range(8):
        b, h = c // 2, c % 2
        out[b, h * T:(h + 1) * T, :] = np.asarray(res3.results[c]["out3"]).T
    return out
